# version 01
# baseline (speedup 1.0000x reference)
import numpy as np
import concourse.bass as bass
import concourse.mybir as mybir
from concourse.bass_utils import run_bass_kernel_spmd
from contextlib import ExitStack

F32 = mybir.dt.float32
BF16 = mybir.dt.bfloat16
AF = mybir.ActivationFunctionType
OP = mybir.AluOpType
AX = mybir.AxisListType.X

D = 1024
T = 2048
NB = 8
TAU = 512
NT = T // TAU
NE = 8
CS = 1920


class Eng:

    def __init__(self, nc, stack, eng, name):
        self.eng = eng
        self.sem = stack.enter_context(nc.semaphore("ch_" + name))
        self.cnt = 0
        self.seen = {}

    def wait(self, other, val=None):
        v = other.cnt if val is None else val
        if v <= 0:
            return
        if other is self:
            if self.seen.get(other, 0) >= v:
                return
        elif self.seen.get(other, 0) >= v:
            return
        self.eng.wait_ge(other.sem, v)
        self.seen[other] = v

    def op(self, fname, *a, step=1, **kw):
        inst = getattr(self.eng, fname)(*a, **kw)
        inst.then_inc(self.sem, step)
        self.cnt += step
        return inst

    def dma(self, out, in_):
        return self.op("dma_start", out, in_, step=16)


class Ctx:
    def __init__(self, nc, stack):
        self.nc = nc
        self.stack = stack
        self.TE = Eng(nc, stack, nc.tensor, "te")
        self.VE = Eng(nc, stack, nc.vector, "ve")
        self.SE = Eng(nc, stack, nc.scalar, "se")
        self.SQ = Eng(nc, stack, nc.sync, "sq")
        self.GQ = Eng(nc, stack, nc.gpsimd, "gq")
        self.all = [self.TE, self.VE, self.SE, self.SQ, self.GQ]

    def sb(self, name, shape, dt=F32):
        return self.stack.enter_context(self.nc.sbuf_tensor(name, shape, dt))

    def ps(self, name, shape, dt=F32):
        return self.stack.enter_context(self.nc.psum_tensor(name, shape, dt))

    def barrier(self):
        for e in self.all:
            for f in self.all:
                if e is not f:
                    e.wait(f)



def emit_ln(ctx, scope, x_dram, xn, ones_bf, epsc, tag):
    nc, TE, VE, SE, GQ = ctx.nc, ctx.TE, ctx.VE, ctx.SE, ctx.GQ
    sb = lambda n, s, d=F32: scope.enter_context(nc.sbuf_tensor(tag + n, s, d))
    ps = lambda n, s: scope.enter_context(nc.psum_tensor(tag + n, s, F32))
    xf = [sb("xf%d" % b, [128, TAU]) for b in range(NB)]
    hi = [sb("hi%d" % b, [128, TAU], BF16) for b in range(NB)]
    lo = [sb("lo%d" % b, [128, TAU], BF16) for b in range(NB)]
    sq = sb("sq", [128, TAU])
    shi = [sb("shi%d" % b, [128, TAU], BF16) for b in range(NB)]
    slo = [sb("slo%d" % b, [128, TAU], BF16) for b in range(NB)]
    mean = sb("mean", [128, TAU])
    var = sb("var", [128, TAU])
    sd = sb("sd", [128, TAU])
    rstd = sb("rstd", [128, TAU])
    xc = sb("xc", [128, TAU])
    pm = ps("pm", [128, TAU])
    pv = ps("pv", [128, TAU])
    for t in range(NT):
        tsl = slice(t * TAU, (t + 1) * TAU)
        for b in range(NB):
            GQ.wait(VE)
            GQ.wait(SE)
            GQ.dma(xf[b][:, :], x_dram[b * 128:(b + 1) * 128, tsl])
        for b in range(NB):
            SE.wait(GQ)
            SE.wait(VE)
            SE.op("copy", hi[b][:, :], xf[b][:, :])
            SE.op("square", sq[:, :], xf[b][:, :])
            SE.wait(SE)
            SE.op("copy", shi[b][:, :], sq[:, :])
            VE.wait(SE)
            VE.op("tensor_sub", lo[b][:, :], xf[b][:, :], hi[b][:, :])
            VE.op("tensor_sub", slo[b][:, :], sq[:, :], shi[b][:, :])
        TE.wait(VE)
        TE.wait(SE)
        for b in range(NB):
            TE.op("matmul", pm[:, :], ones_bf[:, :], hi[b][:, :], start=(b == 0), stop=False)
            TE.op("matmul", pm[:, :], ones_bf[:, :], lo[b][:, :], start=False, stop=(b == NB - 1))
        for b in range(NB):
            TE.op("matmul", pv[:, :], ones_bf[:, :], shi[b][:, :], start=(b == 0), stop=False)
            TE.op("matmul", pv[:, :], ones_bf[:, :], slo[b][:, :], start=False, stop=(b == NB - 1))
        VE.wait(TE)
        VE.op("tensor_scalar", mean[:, :], pm[:, :], 1.0 / D, None, OP.mult)
        VE.op("tensor_scalar", sd[:, :], pv[:, :], 1.0 / D, None, OP.mult)
        VE.wait(VE)
        VE.op("tensor_mul", var[:, :], mean[:, :], mean[:, :])
        VE.wait(VE)
        VE.op("tensor_sub", var[:, :], sd[:, :], var[:, :])
        SE.wait(VE)
        SE.op("activation", sd[:, :], var[:, :], AF.Sqrt, bias=epsc[:, :], scale=1.0)
        VE.wait(SE)
        VE.op("reciprocal", rstd[:, :], sd[:, :])
        for b in range(NB):
            VE.wait(VE)
            VE.op("tensor_sub", xc[:, :], xf[b][:, :], mean[:, :])
            VE.wait(VE)
            VE.op("tensor_mul", xn[b][:, tsl], xc[:, :], rstd[:, :])
    ctx.barrier()


def emit_mixer0(ctx, scope, xn, s_h, wk_dram, bkcols, wv_dram, bvcols, rhdiag, bwf, bwb, tag="m0"):
    nc, TE, VE, SE, SQ, GQ = ctx.nc, ctx.TE, ctx.VE, ctx.SE, ctx.SQ, ctx.GQ
    sb = lambda n, s, d=F32: scope.enter_context(nc.sbuf_tensor(tag + n, s, d))
    ps = lambda n, s: scope.enter_context(nc.psum_tensor(tag + n, s, F32))
    wf = [sb("wf%d" % k, [128, 128]) for k in range(NB)]
    wb = [sb("wb%d" % k, [128, 128], BF16) for k in range(NB)]
    K1t = sb("k1t", [128, TAU])
    m = sb("m", [128, T])
    V = sb("V", [128, T])
    A = sb("A", [128, T])
    B = sb("B", [128, T])
    C = sb("C", [128, T])
    G = [sb("G%d" % r, [128, T]) for r in range(4)]
    i0 = sb("i0", [128, 1])
    i1 = sb("i1", [128, 1])
    pk = ps("pk", [128, TAU])
    pd = ps("pd", [128, TAU])
    for b in range(NB):
        csl = slice(b * 128, (b + 1) * 128)
        for k in range(NB):
            SQ.wait(SE)
            SQ.wait(TE)
            SQ.dma(wf[k][:, :], wk_dram[k * 128:(k + 1) * 128, csl])
        for k in range(NB):
            SE.wait(SQ)
            SE.wait(TE)
            SE.op("copy", wb[k][:, :], wf[k][:, :])
        for t in range(NT):
            tsl = slice(t * TAU, (t + 1) * TAU)
            TE.wait(SE)
            for k in range(NB):
                TE.op("matmul", pk[:, :], wb[k][:, :], xn[k][:, tsl], start=(k == 0), stop=(k == NB - 1))
            SE.wait(TE)
            SE.op("activation", K1t[:, :], pk[:, :], AF.Relu, bias=bkcols[:, b:b + 1], scale=1.0)
            TE.wait(SE)
            TE.wait(VE)
            TE.op("matmul", pd[:, :], rhdiag[b][:, :], K1t[:, :], start=True, stop=True)
            VE.wait(TE)
            VE.op("tensor_single_scalar", m[:, tsl], pd[:, :], 0.5, OP.is_gt)
        for k in range(NB):
            SQ.wait(SE)
            SQ.dma(wf[k][:, :], wv_dram[k * 128:(k + 1) * 128, csl])
        for k in range(NB):
            SE.wait(SQ)
            SE.wait(TE)
            SE.op("copy", wb[k][:, :], wf[k][:, :])
        for t in range(NT):
            tsl = slice(t * TAU, (t + 1) * TAU)
            TE.wait(SE)
            for k in range(NB):
                TE.op("matmul", pk[:, :], wb[k][:, :], xn[k][:, tsl], start=(k == 0), stop=(k == NB - 1))
            SE.wait(TE)
            SE.op("activation", V[:, tsl], pk[:, :], AF.Relu, bias=bvcols[:, b:b + 1], scale=1.0)
        VE.wait(SE)
        VE.op("memset", m[:, 0:1], 0.0)
        VE.wait(VE)
        VE.op("tensor_scalar", A[:, :], m[:, :], -1.0, 1.0, OP.mult, OP.add)
        VE.wait(VE)
        VE.op("tensor_mul", i0[:, :], m[:, T - 1:T], V[:, T - 1:T])
        VE.op("tensor_mul", i1[:, :], A[:, T - 1:T], V[:, 0:1])
        VE.wait(VE)
        VE.op("tensor_add", i0[:, :], i0[:, :], i1[:, :])
        VE.op("tensor_mul", B[:, :], m[:, :], V[:, :])
        VE.wait(VE)
        VE.op("tensor_tensor_scan", G[0][:, :], A[:, :], B[:, :], V[:, 0:1], OP.mult, OP.add)
        for r in range(1, 4):
            VE.wait(VE)
            VE.op("memset", B[:, 0:1], 0.0)
            VE.op("tensor_mul", B[:, 1:T], m[:, 1:T], G[r - 1][:, 0:T - 1])
            VE.wait(VE)
            VE.op("tensor_tensor_scan", G[r][:, :], A[:, :], B[:, :], V[:, 0:1], OP.mult, OP.add)
        VE.wait(VE)
        VE.op("tensor_scalar", C[:, :], G[0][:, :], bwf[b][:, 0:1], None, OP.mult)
        for r in range(1, 4):
            VE.wait(VE)
            VE.op("scalar_tensor_tensor", C[:, :], G[r][:, :], bwf[b][:, r:r + 1], C[:, :], OP.mult, OP.add)
        mrev = m[:, ::-1]
        VE.wait(VE)
        VE.op("tensor_scalar", A[:, :], mrev, -1.0, 1.0, OP.mult, OP.add)
        VE.wait(VE)
        VE.op("memset", A[:, 0:1], 1.0)
        VE.op("memset", B[:, 0:1], 0.0)
        VE.op("memset", B[:, T - 1:T], 0.0)
        VE.wait(VE)
        VE.op("tensor_mul", B[:, 1:T - 1], mrev[:, 1:T - 1], V[:, T - 1:1:-1])
        VE.wait(VE)
        VE.op("tensor_tensor_scan", G[0][:, :], A[:, :], B[:, :], i0[:, :], OP.mult, OP.add)
        for r in range(1, 4):
            VE.wait(VE)
            VE.op("memset", B[:, 0:1], 0.0)
            VE.op("tensor_mul", B[:, 1:T], mrev[:, 1:T], G[r - 1][:, 0:T - 1])
            VE.wait(VE)
            VE.op("tensor_tensor_scan", G[r][:, :], A[:, :], B[:, :], V[:, 0:1], OP.mult, OP.add)
        VE.wait(VE)
        VE.op("tensor_scalar", B[:, :], G[0][:, :], bwb[b][:, 0:1], None, OP.mult)
        for r in range(1, 4):
            VE.wait(VE)
            VE.op("scalar_tensor_tensor", B[:, :], G[r][:, :], bwb[b][:, r:r + 1], B[:, :], OP.mult, OP.add)
        VE.wait(VE)
        VE.op("tensor_add", C[:, :], C[:, :], B[:, ::-1])
        VE.wait(VE)
        VE.op("tensor_reduce", i0[:, :], bwf[b][:, :], AX, OP.add)
        VE.op("tensor_reduce", i1[:, :], bwb[b][:, :], AX, OP.add)
        VE.wait(VE)
        VE.op("tensor_add", i0[:, :], i0[:, :], i1[:, :])
        VE.wait(VE)
        VE.op("tensor_mul", C[:, 0:1], i0[:, :], V[:, 0:1])
        GQ.wait(VE)
        GQ.dma(s_h[b * 128:(b + 1) * 128, :], C[:, :])
    ctx.barrier()


def emit_gate(ctx, scope, xn, gw_dram, brcol, cwR, eye8, i128, tag, dbgl=None):
    nc, TE, VE, SE, SQ, GQ = ctx.nc, ctx.TE, ctx.VE, ctx.SE, ctx.SQ, ctx.GQ
    sb = lambda n, s, d=F32: scope.enter_context(nc.sbuf_tensor(tag + n, s, d))
    ps = lambda n, s: scope.enter_context(nc.psum_tensor(tag + n, s, F32))
    gwf = [sb("gwf%d" % k, [128, 8]) for k in range(NB)]
    gwb = [sb("gwb%d" % k, [128, 8], BF16) for k in range(NB)]
    Lsb = sb("Lsb", [8, TAU])
    Lt = sb("Lt", [128, 8])
    m1 = sb("m1", [128, 1])
    nm1 = sb("nm1", [128, 1])
    e1 = sb("e1", [128, 8])
    ismax = sb("ismax", [128, 8])
    Lt2 = sb("Lt2", [128, 8])
    m2 = sb("m2", [128, 1])
    ind = sb("ind", [128, 8])
    q = sb("q", [128, 8])
    dd = sb("dd", [128, 1])
    rd = sb("rd", [128, 1])
    cwt = sb("cwt", [128, 8])
    pg = ps("pg", [8, TAU])
    pt = ps("pt", [128, 8])
    pr = ps("pr", [8, 128])
    for k in range(NB):
        SQ.dma(gwf[k][:, :], gw_dram[k * 128:(k + 1) * 128, :])
    for k in range(NB):
        SE.wait(SQ)
        SE.op("copy", gwb[k][:, :], gwf[k][:, :])
    for t in range(NT):
        tsl = slice(t * TAU, (t + 1) * TAU)
        TE.wait(SE)
        for k in range(NB):
            TE.op("matmul", pg[:, :], gwb[k][:, :], xn[k][:, tsl], start=(k == 0), stop=(k == NB - 1))
        SE.wait(TE)
        if brcol is None:
            SE.op("copy", Lsb[:, :], pg[:, :])
        else:
            SE.op("add", Lsb[:, :], pg[:, :], brcol[:, :])
        if dbgl is not None:
            GQ.wait(SE)
            GQ.dma(dbgl["dlog0"][:, tsl], Lsb[:, :])
            SE.wait(GQ)
        for j in range(TAU // 128):
            tt = t * TAU + j * 128
            TE.wait(SE)
            TE.wait(VE, VE.cnt)
            TE.op("matmul", pt[:, :], Lsb[:, j * 128:(j + 1) * 128], eye8[:, :], start=True, stop=True)
            VE.wait(TE)
            VE.wait(SE)
            VE.op("tensor_copy", Lt[:, :], pt[:, :])
            VE.wait(VE)
            VE.op("tensor_reduce", m1[:, :], Lt[:, :], AX, OP.max)
            VE.wait(VE)
            VE.op("tensor_scalar", nm1[:, :], m1[:, :], -1.0, None, OP.mult)
            VE.op("tensor_scalar", ismax[:, :], Lt[:, :], m1[:, :], None, OP.is_ge)
            SE.wait(VE)
            SE.op("activation", e1[:, :], Lt[:, :], AF.Exp, bias=nm1[:, :], scale=1.0)
            VE.wait(VE)
            VE.op("scalar_tensor_tensor", Lt2[:, :], ismax[:, :], -1e30, Lt[:, :], OP.mult, OP.add)
            VE.wait(VE)
            VE.op("tensor_reduce", m2[:, :], Lt2[:, :], AX, OP.max)
            VE.wait(VE)
            VE.op("tensor_scalar", ind[:, :], Lt[:, :], m2[:, :], None, OP.is_ge)
            VE.wait(VE)
            VE.wait(SE)
            VE.op("tensor_mul", q[:, :], ind[:, :], e1[:, :])
            VE.wait(VE)
            VE.op("tensor_reduce", dd[:, :], q[:, :], AX, OP.add)
            VE.wait(VE)
            VE.op("reciprocal", rd[:, :], dd[:, :])
            VE.wait(VE)
            VE.op("tensor_scalar", cwt[:, :], q[:, :], rd[:, :], None, OP.mult)
            if dbgl is not None and t == 0 and j == 0:
                GQ.wait(VE)
                GQ.wait(SE)
                for nmx, tx in (("dLt", Lt), ("dm1", m1), ("de1", e1), ("dind", ind),
                                ("dq", q), ("ddd", dd), ("dcwt", cwt), ("dLt2", Lt2), ("dm2", m2)):
                    GQ.dma(dbgl[nmx][:, :], tx[:, :])
                VE.wait(GQ)
                SE.wait(GQ)
            TE.wait(VE)
            TE.wait(SE)
            TE.op("matmul", pr[:, :], cwt[:, :], i128[:, :], start=True, stop=True)
            SE.wait(TE)
            SE.op("copy", cwR[:, tt:tt + 128], pr[:, :])
    ctx.barrier()


def emit_moe(ctx, scope, xn, acc, cwR, w1_dram, w2_dram, sel_f, layer, b1cols=None, b2cols=None, tag="ffn"):
    nc, TE, VE, SE, SQ, GQ = ctx.nc, ctx.TE, ctx.VE, ctx.SE, ctx.SQ, ctx.GQ
    sb = lambda n, s, d=F32: scope.enter_context(nc.sbuf_tensor(tag + n, s, d))
    ps = lambda n, s: scope.enter_context(nc.psum_tensor(tag + n, s, F32))
    NCB = 8
    w1f = [sb("w1f%d" % k, [128, 512]) for k in range(NB)]
    w1ba = [sb("w1a%d" % k, [128, 1024], BF16) for k in range(NB)]
    w1bg = [sb("w1g%d" % k, [128, 1024], BF16) for k in range(NB)] if layer == 1 else None
    w2f = [sb("w2f%d" % k, [128, 256]) for k in range(NCB)]
    w2b = [sb("w2b%d" % k, [128, 256], BF16) for k in range(NCB)]
    hsb = [sb("h%d" % c, [128, T], BF16) for c in range(NCB)]
    cwB = sb("cwB", [128, T])
    gt = sb("gt", [128, TAU])
    tmp = sb("tmp", [128, TAU])
    ph = [ps("ph%d" % i, [128, TAU]) for i in range(2)]
    po = [ps("po%d" % i, [128, TAU]) for i in range(2)]
    pb = ps("pb", [128, TAU])

    for d in range(NB):
        VE.wait(SE)
        VE.op("memset", acc[d][:, :], 0.0)

    last_evac_h = [0, 0]
    last_evac_o = [(None, 0), (None, 0)]
    ih = 0
    io = 0
    nq = 4 if layer == 0 else 2
    for e in range(NE):
        for t in range(NT):
            tsl = slice(t * TAU, (t + 1) * TAU)
            TE.wait(SE)
            TE.wait(VE)
            TE.op("matmul", pb[:, :], sel_f[e][:, :], cwR[:, tsl], start=True, stop=True)
            SE.wait(TE)
            SE.op("copy", cwB[:, tsl], pb[:, :])
        for q in range(nq):
            col0 = q * 1024
            targets = [(w1ba, col0)] if layer == 0 else [(w1ba, col0), (w1bg, 2048 + col0)]
            for dstbuf, c0 in targets:
                for sub in range(2):
                    for k in range(NB):
                        SQ.wait(SE)
                        SQ.dma(w1f[k][:, :], w1_dram[e, k * 128:(k + 1) * 128,
                                                     c0 + sub * 512:c0 + (sub + 1) * 512])
                    for k in range(NB):
                        SE.wait(SQ)
                        SE.wait(TE)
                        SE.op("copy", dstbuf[k][:, sub * 512:(sub + 1) * 512], w1f[k][:, :])
            cast_se = SE.cnt
            for ct in range(NCB):
                bcol_a = q * 8 + ct
                for t in range(NT):
                    tsl = slice(t * TAU, (t + 1) * TAU)
                    TE.wait(SE, max(cast_se, last_evac_h[ih]))
                    for k in range(NB):
                        TE.op("matmul", ph[ih][:, :], w1ba[k][:, ct * 128:(ct + 1) * 128], xn[k][:, tsl],
                              start=(k == 0), stop=(k == NB - 1))
                    SE.wait(TE)
                    if layer == 0:
                        SE.op("activation", hsb[ct][:, tsl], ph[ih][:, :], AF.Gelu)
                        last_evac_h[ih] = SE.cnt
                        ih ^= 1
                    else:
                        SE.op("activation", hsb[ct][:, tsl], ph[ih][:, :], AF.Silu,
                              bias=b1cols[e][:, bcol_a:bcol_a + 1], scale=1.0)
                        last_evac_h[ih] = SE.cnt
                        ih ^= 1
                        TE.wait(SE, max(cast_se, last_evac_h[ih]))
                        for k in range(NB):
                            TE.op("matmul", ph[ih][:, :], w1bg[k][:, ct * 128:(ct + 1) * 128], xn[k][:, tsl],
                                  start=(k == 0), stop=(k == NB - 1))
                        SE.wait(TE)
                        SE.wait(VE)
                        SE.op("add", gt[:, :], ph[ih][:, :], b1cols[e][:, 16 + bcol_a:16 + bcol_a + 1])
                        last_evac_h[ih] = SE.cnt
                        ih ^= 1
                        VE.wait(SE)
                        VE.op("tensor_mul", hsb[ct][:, tsl], hsb[ct][:, tsl], gt[:, :])
            w2row0 = q * 1024
            for dp in range(4):
                for c in range(NCB):
                    SQ.wait(SE)
                    SQ.dma(w2f[c][:, :], w2_dram[e, w2row0 + c * 128: w2row0 + (c + 1) * 128,
                                                 dp * 256:(dp + 1) * 256])
                for c in range(NCB):
                    SE.wait(SQ)
                    SE.wait(TE)
                    SE.op("copy", w2b[c][:, :], w2f[c][:, :])
                cast2_se = SE.cnt
                for dj in range(2):
                    d = dp * 2 + dj
                    for t in range(NT):
                        tsl = slice(t * TAU, (t + 1) * TAU)
                        eng_o, cnt_o = last_evac_o[io]
                        if eng_o is not None:
                            TE.wait(eng_o, cnt_o)
                        TE.wait(SE, cast2_se)
                        TE.wait(VE)
                        for c in range(NCB):
                            TE.op("matmul", po[io][:, :], w2b[c][:, dj * 128:(dj + 1) * 128],
                                  hsb[c][:, tsl], start=(c == 0), stop=(c == NCB - 1))
                        if layer == 0:
                            VE.wait(TE)
                            VE.op("tensor_mul", tmp[:, :], cwB[:, tsl], po[io][:, :])
                            last_evac_o[io] = (VE, VE.cnt)
                        else:
                            SE.wait(TE)
                            SE.wait(VE)
                            SE.op("add", gt[:, :], po[io][:, :], b2cols[e][:, d:d + 1])
                            last_evac_o[io] = (SE, SE.cnt)
                            VE.wait(SE)
                            VE.op("tensor_mul", tmp[:, :], cwB[:, tsl], gt[:, :])
                        VE.wait(VE)
                        VE.op("tensor_add", acc[d][:, tsl], acc[d][:, tsl], tmp[:, :])
                        io ^= 1
    ctx.barrier()


def emit_rowattn1(ctx, scope, xn, wvT_dram, woT_dram, orow, tag="ra"):
    nc, TE, SE, SQ = ctx.nc, ctx.TE, ctx.SE, ctx.SQ
    sb = lambda n, s, d=F32: scope.enter_context(nc.sbuf_tensor(tag + n, s, d))
    ps = lambda n, s: scope.enter_context(nc.psum_tensor(tag + n, s, F32))
    wfr = [sb("wf%d" % k, [128, 1024]) for k in range(NB)]
    wbr = [sb("wb%d" % k, [128, 1024], BF16) for k in range(NB)]
    vrow = [sb("vr%d" % m, [128, 1], BF16) for m in range(NB)]
    pv = ps("pv", [128, 1])
    for k in range(NB):
        SQ.dma(wfr[k][:, :], wvT_dram[k * 128:(k + 1) * 128, :])
    for k in range(NB):
        SE.wait(SQ)
        SE.op("copy", wbr[k][:, :], wfr[k][:, :])
    for mb in range(NB):
        TE.wait(SE)
        for k in range(NB):
            TE.op("matmul", pv[:, :], wbr[k][:, mb * 128:(mb + 1) * 128], xn[k][:, CS:CS + 1],
                  start=(k == 0), stop=(k == NB - 1))
        SE.wait(TE)
        SE.op("copy", vrow[mb][:, :], pv[:, :])
    for k in range(NB):
        SQ.wait(SE)
        SQ.dma(wfr[k][:, :], woT_dram[k * 128:(k + 1) * 128, :])
    for k in range(NB):
        SE.wait(SQ)
        SE.wait(TE)
        SE.op("copy", wbr[k][:, :], wfr[k][:, :])
    for mb in range(NB):
        TE.wait(SE)
        for k in range(NB):
            TE.op("matmul", pv[:, :], wbr[k][:, mb * 128:(mb + 1) * 128], vrow[k][:, :],
                  start=(k == 0), stop=(k == NB - 1))
        SE.wait(TE)
        SE.op("copy", orow[mb][:, :], pv[:, :])
    ctx.barrier()


def emit_combine(ctx, scope, acc, s_src, x_src, out_dram, orow, xn, layer, tag):
    nc, VE, SE, GQ = ctx.nc, ctx.VE, ctx.SE, ctx.GQ
    sb = lambda n, s, d=F32: scope.enter_context(nc.sbuf_tensor(tag + n, s, d))
    xt = sb("xt", [128, TAU])
    st = sb("st", [128, TAU])
    gt2 = sb("gt2", [128, TAU])
    pr = sb("pr", [128, TAU])
    for t in range(NT):
        tsl = slice(t * TAU, (t + 1) * TAU)
        for d in range(NB):
            rsl = slice(d * 128, (d + 1) * 128)
            GQ.wait(VE)
            GQ.dma(xt[:, :], x_src[rsl, tsl])
            if layer == 0:
                GQ.dma(st[:, :], s_src[rsl, tsl])
            SE.wait(VE)
            SE.op("activation", gt2[:, :], acc[d][:, tsl], AF.Silu)
            VE.wait(SE)
            VE.wait(GQ)
            if layer == 0:
                VE.op("tensor_mul", pr[:, :], st[:, :], gt2[:, :])
            else:
                if t < 3:
                    VE.op("tensor_mul", pr[:, :], xn[d][:, tsl], gt2[:, :])
                else:
                    VE.op("tensor_mul", pr[:, 0:CS - 1536], xn[d][:, 1536:CS], gt2[:, 0:CS - 1536])
                    VE.op("tensor_scalar", pr[:, CS - 1536:TAU], gt2[:, CS - 1536:TAU],
                          orow[d][:, :], None, OP.mult)
            VE.wait(VE)
            VE.op("tensor_add", pr[:, :], xt[:, :], pr[:, :])
            GQ.wait(VE)
            GQ.dma(out_dram[rsl, tsl], pr[:, :])
    ctx.barrier()



def build_program(debug=False):
    nc = bass.Bass(trn_type="TRN2", target_bir_lowering=False)
    di = lambda n, s: nc.dram_tensor(n, s, F32, kind="ExternalInput")
    xT = di("xT", [D, T])
    wk0 = di("wk0", [D, D]); wv0 = di("wv0", [D, D])
    bkc = di("bkc", [128, NB]); bvc = di("bvc", [128, NB])
    rhd = di("rhd", [NB, 128, 128])
    bwf_d = di("bwf", [NB, 128, 4]); bwb_d = di("bwb", [NB, 128, 4])
    gw0 = di("gw0", [D, NE])
    w10 = di("w10", [NE, D, 4096]); w20 = di("w20", [NE, 4096, D])
    wvT1 = di("wvT1", [D, D]); woT1 = di("woT1", [D, D])
    w11 = di("w11", [NE, D, 4096]); b1c = di("b1c", [NE, 128, 32])
    w21 = di("w21", [NE, 2048, D]); b2c = di("b2c", [NE, 128, 8])
    wr1 = di("wr1", [D, NE]); br1 = di("br1", [NE, 1])
    ones_d = di("ones_d", [128, 128])
    eye8_d = di("eye8_d", [8, 8])
    eps_d = di("eps_d", [128, 1])
    i128_d = di("i128_d", [128, 128])
    sel_d = di("sel_d", [NE, 8, 128])

    dk = "ExternalOutput" if debug else "Internal"
    s_h = nc.dram_tensor("s_h", [D, T], F32, kind=dk)
    x1_h = nc.dram_tensor("x1_h", [D, T], F32, kind=dk)
    if debug:
        dbg = {n: nc.dram_tensor(n, [D, T], BF16, kind="ExternalOutput")
               for n in ("dxn0", "dacc0", "dxn1", "dacc1")}
        dbg["dcw0"] = nc.dram_tensor("dcw0", [8, T], F32, kind="ExternalOutput")
        dbg["dlog0"] = nc.dram_tensor("dlog0", [8, T], F32, kind="ExternalOutput")
        for nmx, shp in (("dLt", [128, 8]), ("dm1", [128, 1]), ("de1", [128, 8]), ("dind", [128, 8]),
                         ("dq", [128, 8]), ("ddd", [128, 1]), ("dcwt", [128, 8]), ("dLt2", [128, 8]), ("dm2", [128, 1])):
            dbg[nmx] = nc.dram_tensor(nmx, shp, F32, kind="ExternalOutput")
        dbg["dcw1"] = nc.dram_tensor("dcw1", [8, T], F32, kind="ExternalOutput")
        dbg["dorow"] = nc.dram_tensor("dorow", [D, 1], F32, kind="ExternalOutput")
    out = nc.dram_tensor("out", [D, T], F32, kind="ExternalOutput")

    with ExitStack() as stack:
        ctx = Ctx(nc, stack)
        TE, VE, SE, SQ, GQ = ctx.TE, ctx.VE, ctx.SE, ctx.SQ, ctx.GQ
        xn = [ctx.sb("xn%d" % b, [128, T], BF16) for b in range(NB)]
        acc = [ctx.sb("acc%d" % b, [128, T], BF16) for b in range(NB)]
        cwR = ctx.sb("cwR", [8, T])
        ones_bf = ctx.sb("ones_bf", [128, 128], BF16)
        eye8 = ctx.sb("eye8", [8, 8])
        epsc = ctx.sb("epsc", [128, 1])
        i128 = ctx.sb("i128", [128, 128])
        sel_f = [ctx.sb("sel%d" % e, [8, 128]) for e in range(NE)]
        rhdiag = [ctx.sb("rhd%d" % b, [128, 128]) for b in range(NB)]
        bwf = [ctx.sb("bwf%d" % b, [128, 4]) for b in range(NB)]
        bwb = [ctx.sb("bwb%d" % b, [128, 4]) for b in range(NB)]
        bkcols = ctx.sb("bkcols", [128, NB])
        bvcols = ctx.sb("bvcols", [128, NB])
        b1cols = [ctx.sb("b1c%d" % e, [128, 32]) for e in range(NE)]
        b2cols = [ctx.sb("b2c%d" % e, [128, 8]) for e in range(NE)]
        brcol = ctx.sb("brcol", [8, 1])
        orow = [ctx.sb("orow%d" % b, [128, 1]) for b in range(NB)]

        onesf = ctx.sb("onesf", [128, 128])
        GQ.dma(onesf[:, :], ones_d[:, :])
        GQ.dma(eye8[:, :], eye8_d[:, :])
        GQ.dma(epsc[:, :], eps_d[:, :])
        GQ.dma(i128[:, :], i128_d[:, :])
        for e in range(NE):
            GQ.dma(sel_f[e][:, :], sel_d[e, :, :])
            GQ.dma(b1cols[e][:, :], b1c[e, :, :])
            GQ.dma(b2cols[e][:, :], b2c[e, :, :])
        for b in range(NB):
            GQ.dma(rhdiag[b][:, :], rhd[b, :, :])
            GQ.dma(bwf[b][:, :], bwf_d[b, :, :])
            GQ.dma(bwb[b][:, :], bwb_d[b, :, :])
        GQ.dma(bkcols[:, :], bkc[:, :])
        GQ.dma(bvcols[:, :], bvc[:, :])
        GQ.dma(brcol[:, :], br1[:, :])
        SE.wait(GQ)
        SE.op("copy", ones_bf[:, :], onesf[:, :])
        ctx.barrier()

        with ExitStack() as sc:
            emit_ln(ctx, sc, xT, xn, ones_bf, epsc, "ln0")
        if debug:
            for b in range(NB):
                ctx.GQ.dma(dbg["dxn0"][b * 128:(b + 1) * 128, :], xn[b][:, :])
            ctx.barrier()
        with ExitStack() as sc:
            emit_mixer0(ctx, sc, xn, s_h, wk0, bkcols, wv0, bvcols, rhdiag, bwf, bwb)
        with ExitStack() as sc:
            emit_gate(ctx, sc, xn, gw0, None, cwR, eye8, i128, "g0", dbgl=(dbg if debug else None))
        if debug:
            ctx.GQ.dma(dbg["dcw0"][:, :], cwR[:, :])
            ctx.barrier()
        with ExitStack() as sc:
            emit_moe(ctx, sc, xn, acc, cwR, w10, w20, sel_f, layer=0, tag="f0")
        if debug:
            for b in range(NB):
                ctx.GQ.dma(dbg["dacc0"][b * 128:(b + 1) * 128, :], acc[b][:, :])
            ctx.barrier()
        with ExitStack() as sc:
            emit_combine(ctx, sc, acc, s_h, xT, x1_h, None, None, layer=0, tag="c0")

        with ExitStack() as sc:
            emit_ln(ctx, sc, x1_h, xn, ones_bf, epsc, "ln1")
        if debug:
            for b in range(NB):
                ctx.GQ.dma(dbg["dxn1"][b * 128:(b + 1) * 128, :], xn[b][:, :])
            ctx.barrier()
        with ExitStack() as sc:
            emit_rowattn1(ctx, sc, xn, wvT1, woT1, orow)
        if debug:
            for b in range(NB):
                ctx.GQ.dma(dbg["dorow"][b * 128:(b + 1) * 128, :], orow[b][:, :])
            ctx.barrier()
        with ExitStack() as sc:
            emit_gate(ctx, sc, xn, wr1, brcol, cwR, eye8, i128, "g1")
        if debug:
            ctx.GQ.dma(dbg["dcw1"][:, :], cwR[:, :])
            ctx.barrier()
        with ExitStack() as sc:
            emit_moe(ctx, sc, xn, acc, cwR, w11, w21, sel_f, layer=1,
                     b1cols=b1cols, b2cols=b2cols, tag="f1")
        if debug:
            for b in range(NB):
                ctx.GQ.dma(dbg["dacc1"][b * 128:(b + 1) * 128, :], acc[b][:, :])
            ctx.barrier()
        with ExitStack() as sc:
            emit_combine(ctx, sc, acc, None, x1_h, out, orow, xn, layer=1, tag="c1")

        ctx.barrier()
    return nc


def host_inputs(inputs):
    g = {}
    g["wk0"] = np.ascontiguousarray(inputs["L0_attn_Wk"], np.float32)
    g["wv0"] = np.ascontiguousarray(inputs["L0_attn_Wv"], np.float32)
    g["bkc"] = np.ascontiguousarray(inputs["L0_attn_bk"].reshape(NB, 128).T, np.float32)
    g["bvc"] = np.ascontiguousarray(inputs["L0_attn_bv"].reshape(NB, 128).T, np.float32)
    RH = inputs["L0_attn_ReadingHead"]
    rhd = np.zeros((NB, 128, 128), np.float32)
    for b in range(NB):
        for km in range(2):
            h = 2 * b + km
            blk = np.zeros((64, 64), np.float32)
            blk[:, :] = RH[h][:, None]
            rhd[b, km * 64:(km + 1) * 64, km * 64:(km + 1) * 64] = blk
    g["rhd"] = rhd
    bw = inputs["L0_attn_bidirection_weight"]
    bwf = np.zeros((NB, 128, 4), np.float32)
    bwb = np.zeros((NB, 128, 4), np.float32)
    for b in range(NB):
        for km in range(2):
            h = 2 * b + km
            bwf[b, km * 64:(km + 1) * 64, :] = bw[h, :4][None, :]
            bwb[b, km * 64:(km + 1) * 64, :] = bw[h, 4:][None, :]
    g["bwf"] = bwf
    g["bwb"] = bwb
    g["gw0"] = np.ascontiguousarray(inputs["L0_ffn_gate_w"], np.float32)
    g["w10"] = np.ascontiguousarray(inputs["L0_ffn_w1"], np.float32)
    g["w20"] = np.ascontiguousarray(inputs["L0_ffn_w2"], np.float32)
    g["wvT1"] = np.ascontiguousarray(inputs["L1_attn_Wv"].T, np.float32)
    g["woT1"] = np.ascontiguousarray(inputs["L1_attn_Wo"].T, np.float32)
    g["w11"] = np.ascontiguousarray(inputs["L1_ffn_W1"], np.float32)
    g["b1c"] = np.ascontiguousarray(inputs["L1_ffn_b1"].reshape(NE, 32, 128).transpose(0, 2, 1), np.float32)
    g["w21"] = np.ascontiguousarray(inputs["L1_ffn_W2"], np.float32)
    g["b2c"] = np.ascontiguousarray(inputs["L1_ffn_b2"].reshape(NE, 8, 128).transpose(0, 2, 1), np.float32)
    g["wr1"] = np.ascontiguousarray(inputs["L1_ffn_Wr"], np.float32)
    g["br1"] = np.ascontiguousarray(inputs["L1_ffn_br"].reshape(NE, 1), np.float32)
    g["ones_d"] = np.ones((128, 128), np.float32)
    g["eye8_d"] = np.eye(8, dtype=np.float32)
    g["eps_d"] = np.full((128, 1), 1e-5, np.float32)
    g["i128_d"] = np.eye(128, dtype=np.float32)
    sel = np.zeros((NE, 8, 128), np.float32)
    for e in range(NE):
        sel[e, e, :] = 1.0
    g["sel_d"] = sel
    return g


def kernel(**inputs):
    x = np.asarray(inputs["L0_attn_hidden_states"], np.float32)
    g = host_inputs(inputs)
    nc = build_program()
    in_maps = []
    for c in range(8):
        m = dict(g)
        m["xT"] = np.ascontiguousarray(x[c].T)
        in_maps.append(m)
    res = run_bass_kernel_spmd(nc, in_maps, core_ids=list(range(8)))
    outs = [np.asarray(r["out"]).T for r in res.results]
    return np.ascontiguousarray(np.stack(outs, 0), dtype=np.float32)


# version 02 (final)
# speedup vs baseline: 1.2864x; 1.2864x over previous
import numpy as np
import concourse.bass as bass
import concourse.mybir as mybir
from concourse.bass_utils import run_bass_kernel_spmd
from contextlib import ExitStack

F32 = mybir.dt.float32
BF16 = mybir.dt.bfloat16
AF = mybir.ActivationFunctionType
OP = mybir.AluOpType
AX = mybir.AxisListType.X

D = 1024
T = 2048
NB = 8
TAU = 512
NT = T // TAU
NE = 8
CS = 1920


class Eng:

    def __init__(self, nc, stack, eng, name):
        self.eng = eng
        self.sem = stack.enter_context(nc.semaphore("ch_" + name))
        self.cnt = 0
        self.seen = {}

    def wait(self, other, val=None):
        v = other.cnt if val is None else val
        if v <= 0:
            return
        if other is self:
            if self.seen.get(other, 0) >= v:
                return
        elif self.seen.get(other, 0) >= v:
            return
        self.eng.wait_ge(other.sem, v)
        self.seen[other] = v

    def op(self, fname, *a, step=1, **kw):
        inst = getattr(self.eng, fname)(*a, **kw)
        inst.then_inc(self.sem, step)
        self.cnt += step
        return inst

    def dma(self, out, in_):
        return self.op("dma_start", out, in_, step=16)


class Ctx:
    def __init__(self, nc, stack):
        self.nc = nc
        self.stack = stack
        self.TE = Eng(nc, stack, nc.tensor, "te")
        self.VE = Eng(nc, stack, nc.vector, "ve")
        self.SE = Eng(nc, stack, nc.scalar, "se")
        self.SQ = Eng(nc, stack, nc.sync, "sq")
        self.GQ = Eng(nc, stack, nc.gpsimd, "gq")
        self.all = [self.TE, self.VE, self.SE, self.SQ, self.GQ]

    def sb(self, name, shape, dt=F32):
        return self.stack.enter_context(self.nc.sbuf_tensor(name, shape, dt))

    def ps(self, name, shape, dt=F32):
        return self.stack.enter_context(self.nc.psum_tensor(name, shape, dt))

    def barrier(self):
        for e in self.all:
            for f in self.all:
                if e is not f:
                    e.wait(f)



def emit_ln(ctx, scope, x_dram, xn, ones_bf, epsc, tag):
    nc, TE, VE, SE, GQ = ctx.nc, ctx.TE, ctx.VE, ctx.SE, ctx.GQ
    sb = lambda n, s, d=F32: scope.enter_context(nc.sbuf_tensor(tag + n, s, d))
    ps = lambda n, s: scope.enter_context(nc.psum_tensor(tag + n, s, F32))
    xf = [sb("xf%d" % b, [128, TAU]) for b in range(NB)]
    hi = [sb("hi%d" % b, [128, TAU], BF16) for b in range(NB)]
    lo = [sb("lo%d" % b, [128, TAU], BF16) for b in range(NB)]
    sq = sb("sq", [128, TAU])
    shi = [sb("shi%d" % b, [128, TAU], BF16) for b in range(NB)]
    slo = [sb("slo%d" % b, [128, TAU], BF16) for b in range(NB)]
    mean = sb("mean", [128, TAU])
    var = sb("var", [128, TAU])
    sd = sb("sd", [128, TAU])
    rstd = sb("rstd", [128, TAU])
    xc = sb("xc", [128, TAU])
    pm = ps("pm", [128, TAU])
    pv = ps("pv", [128, TAU])
    for t in range(NT):
        tsl = slice(t * TAU, (t + 1) * TAU)
        for b in range(NB):
            GQ.wait(VE)
            GQ.wait(SE)
            GQ.dma(xf[b][:, :], x_dram[b * 128:(b + 1) * 128, tsl])
        for b in range(NB):
            SE.wait(GQ)
            SE.wait(VE)
            SE.op("copy", hi[b][:, :], xf[b][:, :])
            SE.op("square", sq[:, :], xf[b][:, :])
            SE.wait(SE)
            SE.op("copy", shi[b][:, :], sq[:, :])
            VE.wait(SE)
            VE.op("tensor_sub", lo[b][:, :], xf[b][:, :], hi[b][:, :])
            VE.op("tensor_sub", slo[b][:, :], sq[:, :], shi[b][:, :])
        TE.wait(VE)
        TE.wait(SE)
        for b in range(NB):
            TE.op("matmul", pm[:, :], ones_bf[:, :], hi[b][:, :], start=(b == 0), stop=False)
            TE.op("matmul", pm[:, :], ones_bf[:, :], lo[b][:, :], start=False, stop=(b == NB - 1))
        for b in range(NB):
            TE.op("matmul", pv[:, :], ones_bf[:, :], shi[b][:, :], start=(b == 0), stop=False)
            TE.op("matmul", pv[:, :], ones_bf[:, :], slo[b][:, :], start=False, stop=(b == NB - 1))
        VE.wait(TE)
        VE.op("tensor_scalar", mean[:, :], pm[:, :], 1.0 / D, None, OP.mult)
        VE.op("tensor_scalar", sd[:, :], pv[:, :], 1.0 / D, None, OP.mult)
        VE.wait(VE)
        VE.op("tensor_mul", var[:, :], mean[:, :], mean[:, :])
        VE.wait(VE)
        VE.op("tensor_sub", var[:, :], sd[:, :], var[:, :])
        SE.wait(VE)
        SE.op("activation", sd[:, :], var[:, :], AF.Sqrt, bias=epsc[:, :], scale=1.0)
        VE.wait(SE)
        VE.op("reciprocal", rstd[:, :], sd[:, :])
        for b in range(NB):
            VE.wait(VE)
            VE.op("tensor_sub", xc[:, :], xf[b][:, :], mean[:, :])
            VE.wait(VE)
            VE.op("tensor_mul", xn[b][:, tsl], xc[:, :], rstd[:, :])
    ctx.barrier()


def emit_mixer0(ctx, scope, xn, s_h, wk_dram, bkcols, wv_dram, bvcols, rhdiag, bwf, bwb, tag="m0"):
    nc, TE, VE, SE, SQ, GQ = ctx.nc, ctx.TE, ctx.VE, ctx.SE, ctx.SQ, ctx.GQ
    sb = lambda n, s, d=F32: scope.enter_context(nc.sbuf_tensor(tag + n, s, d))
    ps = lambda n, s: scope.enter_context(nc.psum_tensor(tag + n, s, F32))
    wf = [sb("wf%d" % k, [128, 128]) for k in range(NB)]
    wb = [sb("wb%d" % k, [128, 128], BF16) for k in range(NB)]
    K1t = sb("k1t", [128, TAU])
    m = sb("m", [128, T])
    V = sb("V", [128, T])
    A = sb("A", [128, T])
    B = sb("B", [128, T])
    C = sb("C", [128, T])
    G = [sb("G%d" % r, [128, T]) for r in range(4)]
    i0 = sb("i0", [128, 1])
    i1 = sb("i1", [128, 1])
    pk = ps("pk", [128, TAU])
    pd = ps("pd", [128, TAU])
    for b in range(NB):
        csl = slice(b * 128, (b + 1) * 128)
        for k in range(NB):
            SQ.wait(SE)
            SQ.wait(TE)
            SQ.dma(wf[k][:, :], wk_dram[k * 128:(k + 1) * 128, csl])
        for k in range(NB):
            SE.wait(SQ)
            SE.wait(TE)
            SE.op("copy", wb[k][:, :], wf[k][:, :])
        for t in range(NT):
            tsl = slice(t * TAU, (t + 1) * TAU)
            TE.wait(SE)
            for k in range(NB):
                TE.op("matmul", pk[:, :], wb[k][:, :], xn[k][:, tsl], start=(k == 0), stop=(k == NB - 1))
            SE.wait(TE)
            SE.op("activation", K1t[:, :], pk[:, :], AF.Relu, bias=bkcols[:, b:b + 1], scale=1.0)
            TE.wait(SE)
            TE.wait(VE)
            TE.op("matmul", pd[:, :], rhdiag[b][:, :], K1t[:, :], start=True, stop=True)
            VE.wait(TE)
            VE.op("tensor_single_scalar", m[:, tsl], pd[:, :], 0.5, OP.is_gt)
        for k in range(NB):
            SQ.wait(SE)
            SQ.dma(wf[k][:, :], wv_dram[k * 128:(k + 1) * 128, csl])
        for k in range(NB):
            SE.wait(SQ)
            SE.wait(TE)
            SE.op("copy", wb[k][:, :], wf[k][:, :])
        for t in range(NT):
            tsl = slice(t * TAU, (t + 1) * TAU)
            TE.wait(SE)
            for k in range(NB):
                TE.op("matmul", pk[:, :], wb[k][:, :], xn[k][:, tsl], start=(k == 0), stop=(k == NB - 1))
            SE.wait(TE)
            SE.op("activation", V[:, tsl], pk[:, :], AF.Relu, bias=bvcols[:, b:b + 1], scale=1.0)
        VE.wait(SE)
        VE.op("memset", m[:, 0:1], 0.0)
        VE.wait(VE)
        VE.op("tensor_scalar", A[:, :], m[:, :], -1.0, 1.0, OP.mult, OP.add)
        VE.wait(VE)
        VE.op("tensor_mul", i0[:, :], m[:, T - 1:T], V[:, T - 1:T])
        VE.op("tensor_mul", i1[:, :], A[:, T - 1:T], V[:, 0:1])
        VE.wait(VE)
        VE.op("tensor_add", i0[:, :], i0[:, :], i1[:, :])
        VE.op("tensor_mul", B[:, :], m[:, :], V[:, :])
        VE.wait(VE)
        VE.op("tensor_tensor_scan", G[0][:, :], A[:, :], B[:, :], V[:, 0:1], OP.mult, OP.add)
        for r in range(1, 4):
            VE.wait(VE)
            VE.op("memset", B[:, 0:1], 0.0)
            VE.op("tensor_mul", B[:, 1:T], m[:, 1:T], G[r - 1][:, 0:T - 1])
            VE.wait(VE)
            VE.op("tensor_tensor_scan", G[r][:, :], A[:, :], B[:, :], V[:, 0:1], OP.mult, OP.add)
        VE.wait(VE)
        VE.op("tensor_scalar", C[:, :], G[0][:, :], bwf[b][:, 0:1], None, OP.mult)
        for r in range(1, 4):
            VE.wait(VE)
            VE.op("scalar_tensor_tensor", C[:, :], G[r][:, :], bwf[b][:, r:r + 1], C[:, :], OP.mult, OP.add)
        mrev = m[:, ::-1]
        VE.wait(VE)
        VE.op("tensor_scalar", A[:, :], mrev, -1.0, 1.0, OP.mult, OP.add)
        VE.wait(VE)
        VE.op("memset", A[:, 0:1], 1.0)
        VE.op("memset", B[:, 0:1], 0.0)
        VE.op("memset", B[:, T - 1:T], 0.0)
        VE.wait(VE)
        VE.op("tensor_mul", B[:, 1:T - 1], mrev[:, 1:T - 1], V[:, T - 1:1:-1])
        VE.wait(VE)
        VE.op("tensor_tensor_scan", G[0][:, :], A[:, :], B[:, :], i0[:, :], OP.mult, OP.add)
        for r in range(1, 4):
            VE.wait(VE)
            VE.op("memset", B[:, 0:1], 0.0)
            VE.op("tensor_mul", B[:, 1:T], mrev[:, 1:T], G[r - 1][:, 0:T - 1])
            VE.wait(VE)
            VE.op("tensor_tensor_scan", G[r][:, :], A[:, :], B[:, :], V[:, 0:1], OP.mult, OP.add)
        VE.wait(VE)
        VE.op("tensor_scalar", B[:, :], G[0][:, :], bwb[b][:, 0:1], None, OP.mult)
        for r in range(1, 4):
            VE.wait(VE)
            VE.op("scalar_tensor_tensor", B[:, :], G[r][:, :], bwb[b][:, r:r + 1], B[:, :], OP.mult, OP.add)
        VE.wait(VE)
        VE.op("tensor_add", C[:, :], C[:, :], B[:, ::-1])
        VE.wait(VE)
        VE.op("tensor_reduce", i0[:, :], bwf[b][:, :], AX, OP.add)
        VE.op("tensor_reduce", i1[:, :], bwb[b][:, :], AX, OP.add)
        VE.wait(VE)
        VE.op("tensor_add", i0[:, :], i0[:, :], i1[:, :])
        VE.wait(VE)
        VE.op("tensor_mul", C[:, 0:1], i0[:, :], V[:, 0:1])
        GQ.wait(VE)
        GQ.dma(s_h[b * 128:(b + 1) * 128, :], C[:, :])
    ctx.barrier()


def emit_gate(ctx, scope, xn, gw_dram, brcol, cwR, eye8, i128, tag, dbgl=None):
    nc, TE, VE, SE, SQ, GQ = ctx.nc, ctx.TE, ctx.VE, ctx.SE, ctx.SQ, ctx.GQ
    sb = lambda n, s, d=F32: scope.enter_context(nc.sbuf_tensor(tag + n, s, d))
    ps = lambda n, s: scope.enter_context(nc.psum_tensor(tag + n, s, F32))
    gwf = [sb("gwf%d" % k, [128, 8]) for k in range(NB)]
    gwb = [sb("gwb%d" % k, [128, 8], BF16) for k in range(NB)]
    Lsb = sb("Lsb", [8, TAU])
    Lt = sb("Lt", [128, 8])
    m1 = sb("m1", [128, 1])
    nm1 = sb("nm1", [128, 1])
    e1 = sb("e1", [128, 8])
    ismax = sb("ismax", [128, 8])
    Lt2 = sb("Lt2", [128, 8])
    m2 = sb("m2", [128, 1])
    ind = sb("ind", [128, 8])
    q = sb("q", [128, 8])
    dd = sb("dd", [128, 1])
    rd = sb("rd", [128, 1])
    cwt = sb("cwt", [128, 8])
    pg = ps("pg", [8, TAU])
    pt = ps("pt", [128, 8])
    pr = ps("pr", [8, 128])
    for k in range(NB):
        SQ.dma(gwf[k][:, :], gw_dram[k * 128:(k + 1) * 128, :])
    for k in range(NB):
        SE.wait(SQ)
        SE.op("copy", gwb[k][:, :], gwf[k][:, :])
    for t in range(NT):
        tsl = slice(t * TAU, (t + 1) * TAU)
        TE.wait(SE)
        for k in range(NB):
            TE.op("matmul", pg[:, :], gwb[k][:, :], xn[k][:, tsl], start=(k == 0), stop=(k == NB - 1))
        SE.wait(TE)
        if brcol is None:
            SE.op("copy", Lsb[:, :], pg[:, :])
        else:
            SE.op("add", Lsb[:, :], pg[:, :], brcol[:, :])
        if dbgl is not None:
            GQ.wait(SE)
            GQ.dma(dbgl["dlog0"][:, tsl], Lsb[:, :])
            SE.wait(GQ)
        for j in range(TAU // 128):
            tt = t * TAU + j * 128
            TE.wait(SE)
            TE.wait(VE, VE.cnt)
            TE.op("matmul", pt[:, :], Lsb[:, j * 128:(j + 1) * 128], eye8[:, :], start=True, stop=True)
            VE.wait(TE)
            VE.wait(SE)
            VE.op("tensor_copy", Lt[:, :], pt[:, :])
            VE.wait(VE)
            VE.op("tensor_reduce", m1[:, :], Lt[:, :], AX, OP.max)
            VE.wait(VE)
            VE.op("tensor_scalar", nm1[:, :], m1[:, :], -1.0, None, OP.mult)
            VE.op("tensor_scalar", ismax[:, :], Lt[:, :], m1[:, :], None, OP.is_ge)
            SE.wait(VE)
            SE.op("activation", e1[:, :], Lt[:, :], AF.Exp, bias=nm1[:, :], scale=1.0)
            VE.wait(VE)
            VE.op("scalar_tensor_tensor", Lt2[:, :], ismax[:, :], -1e30, Lt[:, :], OP.mult, OP.add)
            VE.wait(VE)
            VE.op("tensor_reduce", m2[:, :], Lt2[:, :], AX, OP.max)
            VE.wait(VE)
            VE.op("tensor_scalar", ind[:, :], Lt[:, :], m2[:, :], None, OP.is_ge)
            VE.wait(VE)
            VE.wait(SE)
            VE.op("tensor_mul", q[:, :], ind[:, :], e1[:, :])
            VE.wait(VE)
            VE.op("tensor_reduce", dd[:, :], q[:, :], AX, OP.add)
            VE.wait(VE)
            VE.op("reciprocal", rd[:, :], dd[:, :])
            VE.wait(VE)
            VE.op("tensor_scalar", cwt[:, :], q[:, :], rd[:, :], None, OP.mult)
            if dbgl is not None and t == 0 and j == 0:
                GQ.wait(VE)
                GQ.wait(SE)
                for nmx, tx in (("dLt", Lt), ("dm1", m1), ("de1", e1), ("dind", ind),
                                ("dq", q), ("ddd", dd), ("dcwt", cwt), ("dLt2", Lt2), ("dm2", m2)):
                    GQ.dma(dbgl[nmx][:, :], tx[:, :])
                VE.wait(GQ)
                SE.wait(GQ)
            TE.wait(VE)
            TE.wait(SE)
            TE.op("matmul", pr[:, :], cwt[:, :], i128[:, :], start=True, stop=True)
            SE.wait(TE)
            SE.op("copy", cwR[:, tt:tt + 128], pr[:, :])
    ctx.barrier()


def emit_moe(ctx, scope, xn, acc, cwR, w1_dram, w2_dram, sel_f, layer, b1cols=None, b2cols=None, tag="ffn"):
    nc, TE, VE, SE, SQ, GQ = ctx.nc, ctx.TE, ctx.VE, ctx.SE, ctx.SQ, ctx.GQ
    sb = lambda n, s, d=F32: scope.enter_context(nc.sbuf_tensor(tag + n, s, d))
    ps = lambda n, s: scope.enter_context(nc.psum_tensor(tag + n, s, F32))
    NCB = 8
    w1f = [sb("w1f%d" % k, [128, 512]) for k in range(4)]
    w1ba = [[sb("w1a%d_%d" % (p, k), [128, 1024], BF16) for k in range(NB)] for p in range(2)]
    w1bg = [sb("w1g%d" % k, [128, 1024], BF16) for k in range(NB)] if layer == 1 else None
    w2f = [sb("w2f%d" % k, [128, 256]) for k in range(4)]
    w2b = [[sb("w2b%d_%d" % (p, k), [128, 256], BF16) for k in range(NCB)] for p in range(2)]
    hsb = [sb("h%d" % c, [128, T], BF16) for c in range(NCB)]
    cwB = sb("cwB", [128, T])
    gt = sb("gt", [128, TAU])
    tmp = sb("tmp", [128, TAU])
    ph = [ps("ph%d" % i, [128, TAU]) for i in range(2)]
    po = [ps("po%d" % i, [128, TAU]) for i in range(2)]
    pb = ps("pb", [128, TAU])

    for d in range(NB):
        VE.wait(SE)
        VE.op("memset", acc[d][:, :], 0.0)

    last_evac_h = [0, 0]
    last_evac_o = [(None, 0), (None, 0)]
    ih = 0
    io = 0
    nq = 4 if layer == 0 else 2
    mmA_done = [0, 0]
    mmB_done = [0, 0]
    castA = [0, 0]
    castB = [0, 0]
    qi = 0
    dpg = 0

    def load_w1(e, q, p):
        targets = [(w1ba[p], q * 1024)] if layer == 0 else [(w1ba[p], q * 1024), (w1bg, 2048 + q * 1024)]
        for dstbuf, c0 in targets:
            for sub in range(2):
                for kb in range(2):
                    for kk in range(4):
                        k = kb * 4 + kk
                        SQ.wait(SE)
                        SQ.dma(w1f[kk][:, :], w1_dram[e, k * 128:(k + 1) * 128,
                                                      c0 + sub * 512:c0 + (sub + 1) * 512])
                    for kk in range(4):
                        k = kb * 4 + kk
                        SE.wait(SQ)
                        SE.wait(TE, max(mmA_done) if dstbuf is w1bg else mmA_done[p])
                        SE.op("copy", dstbuf[k][:, sub * 512:(sub + 1) * 512], w1f[kk][:, :])
        castA[p] = SE.cnt

    def load_w2(e, q, dp, p):
        w2row0 = q * 1024
        for cb in range(2):
            for cc in range(4):
                c = cb * 4 + cc
                SQ.wait(SE)
                SQ.dma(w2f[cc][:, :], w2_dram[e, w2row0 + c * 128: w2row0 + (c + 1) * 128,
                                              dp * 256:(dp + 1) * 256])
            for cc in range(4):
                c = cb * 4 + cc
                SE.wait(SQ)
                SE.wait(TE, mmB_done[p])
                SE.op("copy", w2b[p][c][:, :], w2f[cc][:, :])
        castB[p] = SE.cnt

    load_w1(0, 0, 0)
    for e in range(NE):
        for t in range(NT):
            tsl = slice(t * TAU, (t + 1) * TAU)
            TE.wait(SE)
            TE.wait(VE)
            TE.op("matmul", pb[:, :], sel_f[e][:, :], cwR[:, tsl], start=True, stop=True)
            SE.wait(TE)
            SE.op("copy", cwB[:, tsl], pb[:, :])
        for q in range(nq):
            p = qi % 2
            ne_, nq_ = (e, q + 1) if q + 1 < nq else (e + 1, 0)
            first_b_of_quarter = True
            for ct in range(NCB):
                bcol_a = q * 8 + ct
                for t in range(NT):
                    tsl = slice(t * TAU, (t + 1) * TAU)
                    TE.wait(SE, max(castA[p], last_evac_h[ih]))
                    for k in range(NB):
                        TE.op("matmul", ph[ih][:, :], w1ba[p][k][:, ct * 128:(ct + 1) * 128], xn[k][:, tsl],
                              start=(k == 0), stop=(k == NB - 1))
                    SE.wait(TE)
                    if layer == 0:
                        SE.op("activation", hsb[ct][:, tsl], ph[ih][:, :], AF.Gelu)
                        last_evac_h[ih] = SE.cnt
                        ih ^= 1
                    else:
                        SE.op("activation", hsb[ct][:, tsl], ph[ih][:, :], AF.Silu,
                              bias=b1cols[e][:, bcol_a:bcol_a + 1], scale=1.0)
                        last_evac_h[ih] = SE.cnt
                        ih ^= 1
                        TE.wait(SE, max(castA[p], last_evac_h[ih]))
                        for k in range(NB):
                            TE.op("matmul", ph[ih][:, :], w1bg[k][:, ct * 128:(ct + 1) * 128], xn[k][:, tsl],
                                  start=(k == 0), stop=(k == NB - 1))
                        SE.wait(TE)
                        SE.wait(VE)
                        SE.op("add", gt[:, :], ph[ih][:, :], b1cols[e][:, 16 + bcol_a:16 + bcol_a + 1])
                        last_evac_h[ih] = SE.cnt
                        ih ^= 1
                        VE.wait(SE)
                        VE.op("tensor_mul", hsb[ct][:, tsl], hsb[ct][:, tsl], gt[:, :])
            mmA_done[p] = TE.cnt
            hA_se = SE.cnt
            hA_ve = VE.cnt
            if ne_ < NE:
                load_w1(ne_, nq_, (qi + 1) % 2)
            load_w2(e, q, 0, dpg % 2)
            for dp in range(4):
                pb2 = dpg % 2
                if dp + 1 < 4:
                    load_w2(e, q, dp + 1, (dpg + 1) % 2)
                for dj in range(2):
                    d = dp * 2 + dj
                    for t in range(NT):
                        tsl = slice(t * TAU, (t + 1) * TAU)
                        eng_o, cnt_o = last_evac_o[io]
                        if eng_o is not None:
                            TE.wait(eng_o, cnt_o)
                        TE.wait(SE, max(castB[pb2], hA_se))
                        TE.wait(VE, hA_ve)
                        for c in range(NCB):
                            TE.op("matmul", po[io][:, :], w2b[pb2][c][:, dj * 128:(dj + 1) * 128],
                                  hsb[c][:, tsl], start=(c == 0), stop=(c == NCB - 1))
                        if layer == 0:
                            VE.wait(TE)
                            VE.op("tensor_mul", tmp[:, :], cwB[:, tsl], po[io][:, :])
                            last_evac_o[io] = (VE, VE.cnt)
                        else:
                            SE.wait(TE)
                            SE.wait(VE)
                            SE.op("add", gt[:, :], po[io][:, :], b2cols[e][:, d:d + 1])
                            last_evac_o[io] = (SE, SE.cnt)
                            VE.wait(SE)
                            VE.op("tensor_mul", tmp[:, :], cwB[:, tsl], gt[:, :])
                        VE.wait(VE)
                        VE.op("tensor_add", acc[d][:, tsl], acc[d][:, tsl], tmp[:, :])
                        io ^= 1
                dpg += 1
            mmB_done[(dpg - 1) % 2] = TE.cnt
            mmB_done[dpg % 2] = max(mmB_done[dpg % 2], TE.cnt - 0)
            qi += 1
    ctx.barrier()


def emit_rowattn1(ctx, scope, xn, wvT_dram, woT_dram, orow, tag="ra"):
    nc, TE, SE, SQ = ctx.nc, ctx.TE, ctx.SE, ctx.SQ
    sb = lambda n, s, d=F32: scope.enter_context(nc.sbuf_tensor(tag + n, s, d))
    ps = lambda n, s: scope.enter_context(nc.psum_tensor(tag + n, s, F32))
    wfr = [sb("wf%d" % k, [128, 1024]) for k in range(NB)]
    wbr = [sb("wb%d" % k, [128, 1024], BF16) for k in range(NB)]
    vrow = [sb("vr%d" % m, [128, 1], BF16) for m in range(NB)]
    pv = ps("pv", [128, 1])
    for k in range(NB):
        SQ.dma(wfr[k][:, :], wvT_dram[k * 128:(k + 1) * 128, :])
    for k in range(NB):
        SE.wait(SQ)
        SE.op("copy", wbr[k][:, :], wfr[k][:, :])
    for mb in range(NB):
        TE.wait(SE)
        for k in range(NB):
            TE.op("matmul", pv[:, :], wbr[k][:, mb * 128:(mb + 1) * 128], xn[k][:, CS:CS + 1],
                  start=(k == 0), stop=(k == NB - 1))
        SE.wait(TE)
        SE.op("copy", vrow[mb][:, :], pv[:, :])
    for k in range(NB):
        SQ.wait(SE)
        SQ.dma(wfr[k][:, :], woT_dram[k * 128:(k + 1) * 128, :])
    for k in range(NB):
        SE.wait(SQ)
        SE.wait(TE)
        SE.op("copy", wbr[k][:, :], wfr[k][:, :])
    for mb in range(NB):
        TE.wait(SE)
        for k in range(NB):
            TE.op("matmul", pv[:, :], wbr[k][:, mb * 128:(mb + 1) * 128], vrow[k][:, :],
                  start=(k == 0), stop=(k == NB - 1))
        SE.wait(TE)
        SE.op("copy", orow[mb][:, :], pv[:, :])
    ctx.barrier()


def emit_combine(ctx, scope, acc, s_src, x_src, out_dram, orow, xn, layer, tag):
    nc, VE, SE, GQ = ctx.nc, ctx.VE, ctx.SE, ctx.GQ
    sb = lambda n, s, d=F32: scope.enter_context(nc.sbuf_tensor(tag + n, s, d))
    xt = sb("xt", [128, TAU])
    st = sb("st", [128, TAU])
    gt2 = sb("gt2", [128, TAU])
    pr = sb("pr", [128, TAU])
    for t in range(NT):
        tsl = slice(t * TAU, (t + 1) * TAU)
        for d in range(NB):
            rsl = slice(d * 128, (d + 1) * 128)
            GQ.wait(VE)
            GQ.dma(xt[:, :], x_src[rsl, tsl])
            if layer == 0:
                GQ.dma(st[:, :], s_src[rsl, tsl])
            SE.wait(VE)
            SE.op("activation", gt2[:, :], acc[d][:, tsl], AF.Silu)
            VE.wait(SE)
            VE.wait(GQ)
            if layer == 0:
                VE.op("tensor_mul", pr[:, :], st[:, :], gt2[:, :])
            else:
                if t < 3:
                    VE.op("tensor_mul", pr[:, :], xn[d][:, tsl], gt2[:, :])
                else:
                    VE.op("tensor_mul", pr[:, 0:CS - 1536], xn[d][:, 1536:CS], gt2[:, 0:CS - 1536])
                    VE.op("tensor_scalar", pr[:, CS - 1536:TAU], gt2[:, CS - 1536:TAU],
                          orow[d][:, :], None, OP.mult)
            VE.wait(VE)
            VE.op("tensor_add", pr[:, :], xt[:, :], pr[:, :])
            GQ.wait(VE)
            GQ.dma(out_dram[rsl, tsl], pr[:, :])
    ctx.barrier()



def build_program(debug=False):
    nc = bass.Bass(trn_type="TRN2", target_bir_lowering=False)
    di = lambda n, s: nc.dram_tensor(n, s, F32, kind="ExternalInput")
    xT = di("xT", [D, T])
    wk0 = di("wk0", [D, D]); wv0 = di("wv0", [D, D])
    bkc = di("bkc", [128, NB]); bvc = di("bvc", [128, NB])
    rhd = di("rhd", [NB, 128, 128])
    bwf_d = di("bwf", [NB, 128, 4]); bwb_d = di("bwb", [NB, 128, 4])
    gw0 = di("gw0", [D, NE])
    w10 = di("w10", [NE, D, 4096]); w20 = di("w20", [NE, 4096, D])
    wvT1 = di("wvT1", [D, D]); woT1 = di("woT1", [D, D])
    w11 = di("w11", [NE, D, 4096]); b1c = di("b1c", [NE, 128, 32])
    w21 = di("w21", [NE, 2048, D]); b2c = di("b2c", [NE, 128, 8])
    wr1 = di("wr1", [D, NE]); br1 = di("br1", [NE, 1])
    ones_d = di("ones_d", [128, 128])
    eye8_d = di("eye8_d", [8, 8])
    eps_d = di("eps_d", [128, 1])
    i128_d = di("i128_d", [128, 128])
    sel_d = di("sel_d", [NE, 8, 128])

    dk = "ExternalOutput" if debug else "Internal"
    s_h = nc.dram_tensor("s_h", [D, T], F32, kind=dk)
    x1_h = nc.dram_tensor("x1_h", [D, T], F32, kind=dk)
    if debug:
        dbg = {n: nc.dram_tensor(n, [D, T], BF16, kind="ExternalOutput")
               for n in ("dxn0", "dacc0", "dxn1", "dacc1")}
        dbg["dcw0"] = nc.dram_tensor("dcw0", [8, T], F32, kind="ExternalOutput")
        dbg["dlog0"] = nc.dram_tensor("dlog0", [8, T], F32, kind="ExternalOutput")
        for nmx, shp in (("dLt", [128, 8]), ("dm1", [128, 1]), ("de1", [128, 8]), ("dind", [128, 8]),
                         ("dq", [128, 8]), ("ddd", [128, 1]), ("dcwt", [128, 8]), ("dLt2", [128, 8]), ("dm2", [128, 1])):
            dbg[nmx] = nc.dram_tensor(nmx, shp, F32, kind="ExternalOutput")
        dbg["dcw1"] = nc.dram_tensor("dcw1", [8, T], F32, kind="ExternalOutput")
        dbg["dorow"] = nc.dram_tensor("dorow", [D, 1], F32, kind="ExternalOutput")
    out = nc.dram_tensor("out", [D, T], F32, kind="ExternalOutput")

    with ExitStack() as stack:
        ctx = Ctx(nc, stack)
        TE, VE, SE, SQ, GQ = ctx.TE, ctx.VE, ctx.SE, ctx.SQ, ctx.GQ
        xn = [ctx.sb("xn%d" % b, [128, T], BF16) for b in range(NB)]
        acc = [ctx.sb("acc%d" % b, [128, T], BF16) for b in range(NB)]
        cwR = ctx.sb("cwR", [8, T])
        ones_bf = ctx.sb("ones_bf", [128, 128], BF16)
        eye8 = ctx.sb("eye8", [8, 8])
        epsc = ctx.sb("epsc", [128, 1])
        i128 = ctx.sb("i128", [128, 128])
        sel_f = [ctx.sb("sel%d" % e, [8, 128]) for e in range(NE)]
        rhdiag = [ctx.sb("rhd%d" % b, [128, 128]) for b in range(NB)]
        bwf = [ctx.sb("bwf%d" % b, [128, 4]) for b in range(NB)]
        bwb = [ctx.sb("bwb%d" % b, [128, 4]) for b in range(NB)]
        bkcols = ctx.sb("bkcols", [128, NB])
        bvcols = ctx.sb("bvcols", [128, NB])
        b1cols = [ctx.sb("b1c%d" % e, [128, 32]) for e in range(NE)]
        b2cols = [ctx.sb("b2c%d" % e, [128, 8]) for e in range(NE)]
        brcol = ctx.sb("brcol", [8, 1])
        orow = [ctx.sb("orow%d" % b, [128, 1]) for b in range(NB)]

        onesf = ctx.sb("onesf", [128, 128])
        GQ.dma(onesf[:, :], ones_d[:, :])
        GQ.dma(eye8[:, :], eye8_d[:, :])
        GQ.dma(epsc[:, :], eps_d[:, :])
        GQ.dma(i128[:, :], i128_d[:, :])
        for e in range(NE):
            GQ.dma(sel_f[e][:, :], sel_d[e, :, :])
            GQ.dma(b1cols[e][:, :], b1c[e, :, :])
            GQ.dma(b2cols[e][:, :], b2c[e, :, :])
        for b in range(NB):
            GQ.dma(rhdiag[b][:, :], rhd[b, :, :])
            GQ.dma(bwf[b][:, :], bwf_d[b, :, :])
            GQ.dma(bwb[b][:, :], bwb_d[b, :, :])
        GQ.dma(bkcols[:, :], bkc[:, :])
        GQ.dma(bvcols[:, :], bvc[:, :])
        GQ.dma(brcol[:, :], br1[:, :])
        SE.wait(GQ)
        SE.op("copy", ones_bf[:, :], onesf[:, :])
        ctx.barrier()

        with ExitStack() as sc:
            emit_ln(ctx, sc, xT, xn, ones_bf, epsc, "ln0")
        if debug:
            for b in range(NB):
                ctx.GQ.dma(dbg["dxn0"][b * 128:(b + 1) * 128, :], xn[b][:, :])
            ctx.barrier()
        with ExitStack() as sc:
            emit_mixer0(ctx, sc, xn, s_h, wk0, bkcols, wv0, bvcols, rhdiag, bwf, bwb)
        with ExitStack() as sc:
            emit_gate(ctx, sc, xn, gw0, None, cwR, eye8, i128, "g0", dbgl=(dbg if debug else None))
        if debug:
            ctx.GQ.dma(dbg["dcw0"][:, :], cwR[:, :])
            ctx.barrier()
        with ExitStack() as sc:
            emit_moe(ctx, sc, xn, acc, cwR, w10, w20, sel_f, layer=0, tag="f0")
        if debug:
            for b in range(NB):
                ctx.GQ.dma(dbg["dacc0"][b * 128:(b + 1) * 128, :], acc[b][:, :])
            ctx.barrier()
        with ExitStack() as sc:
            emit_combine(ctx, sc, acc, s_h, xT, x1_h, None, None, layer=0, tag="c0")

        with ExitStack() as sc:
            emit_ln(ctx, sc, x1_h, xn, ones_bf, epsc, "ln1")
        if debug:
            for b in range(NB):
                ctx.GQ.dma(dbg["dxn1"][b * 128:(b + 1) * 128, :], xn[b][:, :])
            ctx.barrier()
        with ExitStack() as sc:
            emit_rowattn1(ctx, sc, xn, wvT1, woT1, orow)
        if debug:
            for b in range(NB):
                ctx.GQ.dma(dbg["dorow"][b * 128:(b + 1) * 128, :], orow[b][:, :])
            ctx.barrier()
        with ExitStack() as sc:
            emit_gate(ctx, sc, xn, wr1, brcol, cwR, eye8, i128, "g1")
        if debug:
            ctx.GQ.dma(dbg["dcw1"][:, :], cwR[:, :])
            ctx.barrier()
        with ExitStack() as sc:
            emit_moe(ctx, sc, xn, acc, cwR, w11, w21, sel_f, layer=1,
                     b1cols=b1cols, b2cols=b2cols, tag="f1")
        if debug:
            for b in range(NB):
                ctx.GQ.dma(dbg["dacc1"][b * 128:(b + 1) * 128, :], acc[b][:, :])
            ctx.barrier()
        with ExitStack() as sc:
            emit_combine(ctx, sc, acc, None, x1_h, out, orow, xn, layer=1, tag="c1")

        ctx.barrier()
    return nc


def host_inputs(inputs):
    g = {}
    g["wk0"] = np.ascontiguousarray(inputs["L0_attn_Wk"], np.float32)
    g["wv0"] = np.ascontiguousarray(inputs["L0_attn_Wv"], np.float32)
    g["bkc"] = np.ascontiguousarray(inputs["L0_attn_bk"].reshape(NB, 128).T, np.float32)
    g["bvc"] = np.ascontiguousarray(inputs["L0_attn_bv"].reshape(NB, 128).T, np.float32)
    RH = inputs["L0_attn_ReadingHead"]
    rhd = np.zeros((NB, 128, 128), np.float32)
    for b in range(NB):
        for km in range(2):
            h = 2 * b + km
            blk = np.zeros((64, 64), np.float32)
            blk[:, :] = RH[h][:, None]
            rhd[b, km * 64:(km + 1) * 64, km * 64:(km + 1) * 64] = blk
    g["rhd"] = rhd
    bw = inputs["L0_attn_bidirection_weight"]
    bwf = np.zeros((NB, 128, 4), np.float32)
    bwb = np.zeros((NB, 128, 4), np.float32)
    for b in range(NB):
        for km in range(2):
            h = 2 * b + km
            bwf[b, km * 64:(km + 1) * 64, :] = bw[h, :4][None, :]
            bwb[b, km * 64:(km + 1) * 64, :] = bw[h, 4:][None, :]
    g["bwf"] = bwf
    g["bwb"] = bwb
    g["gw0"] = np.ascontiguousarray(inputs["L0_ffn_gate_w"], np.float32)
    g["w10"] = np.ascontiguousarray(inputs["L0_ffn_w1"], np.float32)
    g["w20"] = np.ascontiguousarray(inputs["L0_ffn_w2"], np.float32)
    g["wvT1"] = np.ascontiguousarray(inputs["L1_attn_Wv"].T, np.float32)
    g["woT1"] = np.ascontiguousarray(inputs["L1_attn_Wo"].T, np.float32)
    g["w11"] = np.ascontiguousarray(inputs["L1_ffn_W1"], np.float32)
    g["b1c"] = np.ascontiguousarray(inputs["L1_ffn_b1"].reshape(NE, 32, 128).transpose(0, 2, 1), np.float32)
    g["w21"] = np.ascontiguousarray(inputs["L1_ffn_W2"], np.float32)
    g["b2c"] = np.ascontiguousarray(inputs["L1_ffn_b2"].reshape(NE, 8, 128).transpose(0, 2, 1), np.float32)
    g["wr1"] = np.ascontiguousarray(inputs["L1_ffn_Wr"], np.float32)
    g["br1"] = np.ascontiguousarray(inputs["L1_ffn_br"].reshape(NE, 1), np.float32)
    g["ones_d"] = np.ones((128, 128), np.float32)
    g["eye8_d"] = np.eye(8, dtype=np.float32)
    g["eps_d"] = np.full((128, 1), 1e-5, np.float32)
    g["i128_d"] = np.eye(128, dtype=np.float32)
    sel = np.zeros((NE, 8, 128), np.float32)
    for e in range(NE):
        sel[e, e, :] = 1.0
    g["sel_d"] = sel
    return g


def kernel(**inputs):
    x = np.asarray(inputs["L0_attn_hidden_states"], np.float32)
    g = host_inputs(inputs)
    nc = build_program()
    in_maps = []
    for c in range(8):
        m = dict(g)
        m["xT"] = np.ascontiguousarray(x[c].T)
        in_maps.append(m)
    res = run_bass_kernel_spmd(nc, in_maps, core_ids=list(range(8)))
    outs = [np.asarray(r["out"]).T for r in res.results]
    return np.ascontiguousarray(np.stack(outs, 0), dtype=np.float32)
